# version 07 (final)
# baseline (speedup 1.0000x reference)
import numpy as np
import ml_dtypes
from contextlib import ExitStack
import concourse.bass as bass
import concourse.mybir as mybir
from concourse.bass_utils import run_bass_kernel_spmd

F32 = mybir.dt.float32
BF16 = mybir.dt.bfloat16
AF = mybir.ActivationFunctionType
ALU = mybir.AluOpType
AX = mybir.AxisListType
BF = ml_dtypes.bfloat16

NBLK = 8
EPS = 1e-5
XB, BB, CB, DTB = 8192, 16384, 17408, 18432


def build_program():
    nc = bass.Bass("TRN2", target_bir_lowering=False, debug=False)

    def ein(name, shape, dt=F32):
        return nc.dram_tensor(name, list(shape), dt, kind="ExternalInput").ap()

    x_spans_d = ein("x_spans", (4, 32, 128, 1027))
    valid_d = ein("valid", (128, 24))
    x_tm_d = ein("x_tm", (1024, 4096))
    w_hbc = ein("w_hbc", (81, 128, 32, 128))
    w_gate = ein("w_gate", (16, 2, 128, 8192))
    w_out = ein("w_out", (8, 64, 128, 512))
    w_base = ein("w_base", (8, 32, 128, 512))
    w_rout = ein("w_rout", (128, 32, 8))
    w_A = ein("w_A", (128, 32, 128))
    w_B = ein("w_B", (128, 4096))
    conv_w_d = ein("conv_w", (128, 80, 4))
    conv_b_d = ein("conv_b", (128, 80))
    dt_bias_d = ein("dt_bias", (128, 1))
    A_log_d = ein("A_log", (128, 1))
    D_bc_d = ein("D_bc", (128, 128))
    b_base_d = ein("b_base", (1, 4096))
    norm_w_d = ein("norm_w", (128, 64))
    out_d = nc.dram_tensor("out", [1024, 4096], F32, kind="ExternalOutput").ap()

    x_ssm = nc.dram_tensor("x_ssm", [NBLK, 128, 8192], BF16).ap()
    gate_d = nc.dram_tensor("gate_d", [1024, 8192], BF16).ap()
    y_d = nc.dram_tensor("y_d", [64, 128, 1024], BF16).ap()
    r1_d = nc.dram_tensor("r1_d", [1024, 4096], BF16).ap()

    id_bf = nc.inline_tensor(np.eye(128, dtype=BF), "id_bf").ap()
    id_f32 = nc.inline_tensor(np.eye(128, dtype=np.float32), "id_f32").ap()
    ut01 = nc.inline_tensor(
        np.triu(np.ones((128, 128), dtype=np.float32)), "ut01").ap()
    ind_np = (np.arange(128)[:, None] > np.arange(128)[None, :]).astype(BF)
    ind_bf = nc.inline_tensor(ind_np, "ind_bf").ap()
    ones_bf = nc.inline_tensor(np.ones((1, 128), dtype=BF), "ones_bf").ap()
    ones_f = nc.inline_tensor(np.ones((1, 128), dtype=np.float32), "ones_f").ap()

    from collections import defaultdict
    C = defaultdict(int)

    xs_used = [0]
    xblk_dma = [0]
    gm_pe = [0]
    p4_free = [0]
    diff_used = [0]
    exp_done = [0]
    s_used = [0]
    yoff_evd = [0]
    rbf_ready = [0]
    diff_pe = [0]
    yoff_pe = [0]
    ydiag_pe = [0]
    s_pe = [0]
    gate_used = [0]
    p5_free = [["dv", 0]]
    p5_pe = [0]
    yfm_dma = [0]
    psS_free = [["dv", 0]]
    psS_ready = [0]
    psT_free = [["dv", 0]]
    silu_prev = [0]
    dt_chain_pe = [0]
    dt_chain_dv = [0]
    rj_used = [0]
    exp_ac = {}
    mprime_done = {}
    ydiag_used = {}
    x_used = {}
    ln2t_done = [0]
    sq4_used = [0]
    r1t_ld = [0]
    rt_pe = [0]
    hfm_ev = [0]
    sm2_prev = [0]

    with ExitStack() as top:
        SEM = {}
        for s in ["wq", "xq", "sq", "pe", "dv", "ac", "gp", "cc", "bar"]:
            SEM[s] = top.enter_context(nc.semaphore(s))

        last_eng = {}

        def I(instr, s, n=1):
            eng = instr.ins.engine
            prev = last_eng.get(eng)
            if prev is not None and not (s == "pe" and prev[0] == "pe"):
                instr._wait_ge(SEM[prev[0]], prev[1])
            C[s] += n
            instr.then_inc(SEM[s], n)
            if s != "bar":
                last_eng[eng] = (s, C[s])
            return C[s]

        block = top.enter_context(nc.Block())
        nbar = [0]

        def barrier():
            nbar[0] += 1
            tgt = 5 * nbar[0]
            for name, drain in [("sync", "xq"), ("scalar", "sq"), ("gpsimd", "wq"),
                                ("tensor", None), ("vector", None)]:
                dec = getattr(block, name)
                def _f(e, drain=drain, tgt=tgt):
                    if drain is not None:
                        e.wait_ge(SEM[drain], C[drain])
                    I(e.nop(), "bar")
                    e.wait_ge(SEM["bar"], tgt)
                dec(_f)

        cslast = top.enter_context(nc.sbuf_tensor("cslast", [128, 8], F32))
        gl_col = top.enter_context(nc.sbuf_tensor("gl_col", [128, 1], F32))
        ssq = top.enter_context(nc.sbuf_tensor("ssq", [128, 8], F32))
        rstd = top.enter_context(nc.sbuf_tensor("rstd", [128, 8], F32))
        idb_s = top.enter_context(nc.sbuf_tensor("idb_s", [128, 128], BF16))
        idf_s = top.enter_context(nc.sbuf_tensor("idf_s", [128, 128], F32))
        ut_s = top.enter_context(nc.sbuf_tensor("ut_s", [128, 128], F32))
        ind_s = top.enter_context(nc.sbuf_tensor("ind_s", [128, 128], BF16))
        onesb_s = top.enter_context(nc.sbuf_tensor("onesb_s", [1, 128], BF16))
        onesf_s = top.enter_context(nc.sbuf_tensor("onesf_s", [1, 128], F32))
        cw_s = top.enter_context(nc.sbuf_tensor("cw_s", [128, 320], F32))
        cb_s = top.enter_context(nc.sbuf_tensor("cb_s", [128, 80], F32))
        dtb_s = top.enter_context(nc.sbuf_tensor("dtb_s", [128, 1], F32))
        alog_s = top.enter_context(nc.sbuf_tensor("alog_s", [128, 1], F32))
        expA_s = top.enter_context(nc.sbuf_tensor("expA_s", [128, 1], F32))
        dbc_s = top.enter_context(nc.sbuf_tensor("dbc_s", [128, 128], F32))
        nw_s = top.enter_context(nc.sbuf_tensor("nw_s", [128, 64], F32))
        valid_s = top.enter_context(nc.sbuf_tensor("valid_s", [128, 24], F32))
        eps_col = top.enter_context(nc.sbuf_tensor("eps_col", [128, 1], F32))
        one_col = top.enter_context(nc.sbuf_tensor("one_col", [128, 1], F32))

        scope_ln = ExitStack()
        ln1 = scope_ln.enter_context(nc.sbuf_tensor("ln1", [128, 32 * 1027], BF16, side="right"))

        scope_ssm = ExitStack()
        B_fm = scope_ssm.enter_context(nc.sbuf_tensor("B_fm", [128, 8 * 1024], BF16))
        C_fm = scope_ssm.enter_context(nc.sbuf_tensor("C_fm", [128, 8 * 1024], BF16))
        B_tm = scope_ssm.enter_context(nc.sbuf_tensor("B_tm", [128, 64 * 128], BF16))
        R_bf = scope_ssm.enter_context(nc.sbuf_tensor("R_bf", [128, 8192], BF16))
        dt_tm = scope_ssm.enter_context(nc.sbuf_tensor("dt_tm", [128, 8 * 128], F32))
        A_dt_tm = scope_ssm.enter_context(nc.sbuf_tensor("A_dt_tm", [128, 8 * 128], F32))
        expcs_tm = scope_ssm.enter_context(nc.sbuf_tensor("expcs_tm", [128, 8 * 128], F32))
        dtdecay_tm = scope_ssm.enter_context(nc.sbuf_tensor("dtdecay_tm", [128, 8 * 128], F32))
        gamma_bc = scope_ssm.enter_context(nc.sbuf_tensor("gamma_bc", [128, 8 * 128], F32))
        x_blk = scope_ssm.enter_context(nc.sbuf_tensor("x_blk", [128, 8192], BF16))

        with ExitStack() as phA:
            gcol = phA.enter_context(nc.sbuf_tensor("gcol", [128, 1], F32))
            grow = phA.enter_context(nc.sbuf_tensor("grow", [1, 128], F32))
            onesc = phA.enter_context(nc.sbuf_tensor("onesc", [128, 1], BF16))

            @block.sync
            def _(e):
                for dst, src in [(idb_s, id_bf), (idf_s, id_f32), (ut_s, ut01),
                                 (ind_s, ind_bf), (onesb_s, ones_bf), (onesf_s, ones_f),
                                 (cw_s, conv_w_d.rearrange("p a b -> p (a b)")),
                                 (cb_s, conv_b_d), (dtb_s, dt_bias_d), (alog_s, A_log_d),
                                 (dbc_s, D_bc_d), (nw_s, norm_w_d),
                                 (valid_s, valid_d)]:
                    e.wait_ge(SEM["xq"], C["xq"])
                    I(e.dma_start(out=dst[:, :], in_=src), "xq", 16)
            ld_const = C["xq"]
            @block.vector
            def _(e):
                e.wait_ge(SEM["xq"], ld_const)
                I(e.memset(onesc[:, :], 1.0), "dv")
                I(e.memset(eps_col[:, :], EPS), "dv")
                I(e.memset(one_col[:, :], 1.0), "dv")
                I(e.memset(ssq[:, :], 0.0), "dv")
                I(e.memset(R_bf[:, :], 0.0), "dv")
            init_dv = C["dv"]
            @block.scalar
            def _(e):
                e.wait_ge(SEM["xq"], ld_const)
                I(e.activation(expA_s[:, :], alog_s[:, :], AF.Exp), "ac")
            init_ac = C["ac"]

            sec_end = [0, 0, 0]
            for sec in range(4):
                own = (sec == 3)
                lnstk = ExitStack()
                xrot = lnstk.enter_context(nc.sbuf_tensor(f"xrot{sec}", [128, 2 * 1027], BF16))
                xsqr = lnstk.enter_context(nc.sbuf_tensor(f"xsqr{sec}", [128, 2 * 1027], BF16))
                mrow = lnstk.enter_context(nc.sbuf_tensor(f"mrow{sec}", [1, 1027], F32))
                rrow = lnstk.enter_context(nc.sbuf_tensor(f"rrow{sec}", [1, 1027], F32))
                srow = lnstk.enter_context(nc.sbuf_tensor(f"srow{sec}", [1, 1027], F32))
                ln_scr = lnstk.enter_context(nc.sbuf_tensor(f"ln_scr{sec}", [128, 1027], F32))
                with ExitStack() as pstk:
                    p_sum = pstk.enter_context(nc.psum_tensor(f"p_sum{sec}", [1, 1536], F32))
                    p_sq = pstk.enter_context(nc.psum_tensor(f"p_sq{sec}", [1, 1536], F32))
                    stats_rel = [0] * 32
                    xk_ld = [0] * 32
                    sq_done = [0] * 32
                    for k in range(32):
                        sl = (k % 2) * 1027
                        @block.gpsimd
                        def _(e, k=k, sl=sl, sec=sec):
                            if k >= 2:
                                e.wait_ge(SEM["pe"], stats_rel[k - 2])
                            elif sec > 0 and k == 0:
                                e.wait_ge(SEM["pe"], sec_end[0])
                                e.wait_ge(SEM["dv"], sec_end[1])
                                e.wait_ge(SEM["ac"], sec_end[2])
                            e.wait_ge(SEM["wq"], C["wq"])
                            xk_ld[k] = I(e.dma_start(
                                out=xrot[:, sl:sl + 1027],
                                in_=x_spans_d[sec, k]), "wq", 16)
                        @block.scalar
                        def _(e, k=k, sl=sl):
                            e.wait_ge(SEM["wq"], xk_ld[k])
                            if k >= 2:
                                e.wait_ge(SEM["pe"], stats_rel[k - 2])
                            sq_done[k] = I(e.activation(
                                xsqr[:, sl:sl + 1027], xrot[:, sl:sl + 1027],
                                AF.Square), "ac")
                        @block.tensor
                        def _(e, k=k, sl=sl, sec=sec):
                            if k == 0:
                                if sec == 0:
                                    e.wait_ge(SEM["dv"], init_dv)
                                    e.wait_ge(SEM["ac"], init_ac)
                                else:
                                    e.wait_ge(SEM["dv"], sec_end[1])
                                    e.wait_ge(SEM["ac"], sec_end[2])
                            e.wait_ge(SEM["wq"], xk_ld[k])
                            for (o, w) in [(0, 512), (512, 512), (1024, 3)]:
                                e.matmul(p_sum[:, o:o + w], onesc[:, :],
                                         xrot[:, sl + o:sl + o + w],
                                         start=(k == 0), stop=(k == 31))
                            e.wait_ge(SEM["ac"], sq_done[k])
                            for (o, w) in [(0, 512), (512, 512), (1024, 3)]:
                                mm = e.matmul(p_sq[:, o:o + w], onesc[:, :],
                                              xsqr[:, sl + o:sl + o + w],
                                              start=(k == 0), stop=(k == 31))
                            stats_rel[k] = I(mm, "pe")
                    @block.vector
                    def _(e):
                        e.wait_ge(SEM["pe"], stats_rel[31])
                        I(e.tensor_scalar(mrow[:, :], p_sum[0:1, 0:1027], 1.0 / 4096.0,
                                          None, ALU.mult), "dv")
                        I(e.tensor_scalar(srow[:, :], p_sq[0:1, 0:1027], 1.0 / 4096.0,
                                          None, ALU.mult), "dv")
                        I(e.tensor_tensor(rrow[:, :], mrow[:, :], mrow[:, :], ALU.mult), "dv")
                        I(e.tensor_tensor(srow[:, :], srow[:, :], rrow[:, :], ALU.subtract), "dv")
                    rows1 = C["dv"]
                    @block.scalar
                    def _(e):
                        e.wait_ge(SEM["dv"], rows1)
                        I(e.activation(srow[:, :], srow[:, :], AF.Sqrt,
                                       bias=eps_col[0:1, 0:1]), "ac")
                    sd_done = C["ac"]
                    @block.vector
                    def _(e):
                        e.wait_ge(SEM["ac"], sd_done)
                        I(e.reciprocal(rrow[:, :], srow[:, :]), "dv")
                    rows_done = C["dv"]

                with ExitStack() as pstk:
                    p_mbc = pstk.enter_context(nc.psum_tensor(f"p_mbc{sec}", [128, 1536], F32))
                    p_rbc = pstk.enter_context(nc.psum_tensor(f"p_rbc{sec}", [128, 1536], F32))
                    @block.tensor
                    def _(e):
                        e.wait_ge(SEM["dv"], rows_done)
                        for (o, w) in [(0, 512), (512, 512), (1024, 3)]:
                            e.matmul(p_mbc[:, o:o + w], onesf_s[0:1, :],
                                     mrow[:, o:o + w], start=True, stop=True)
                            mm = e.matmul(p_rbc[:, o:o + w], onesf_s[0:1, :],
                                          rrow[:, o:o + w], start=True, stop=True)
                        I(mm, "pe")
                    bc_done = C["pe"]
                    nrm_rel = [0] * 32
                    for k in range(32):
                        sl = (k % 2) * 1027
                        @block.gpsimd
                        def _(e, k=k, sl=sl, sec=sec):
                            if k >= 2:
                                e.wait_ge(SEM["dv"], nrm_rel[k - 2])
                            else:
                                e.wait_ge(SEM["pe"], stats_rel[30 + k])
                            e.wait_ge(SEM["wq"], C["wq"])
                            xk_ld[k] = I(e.dma_start(
                                out=xrot[:, sl:sl + 1027],
                                in_=x_spans_d[sec, k]), "wq", 16)
                        @block.vector
                        def _(e, k=k, sl=sl):
                            e.wait_ge(SEM["wq"], xk_ld[k])
                            if k == 0:
                                e.wait_ge(SEM["pe"], bc_done)
                            I(e.tensor_tensor(ln_scr[:, :], xrot[:, sl:sl + 1027],
                                              p_mbc[:, 0:1027], ALU.subtract), "dv")
                            nrm_rel[k] = I(e.tensor_tensor(
                                ln1[:, k * 1027:(k + 1) * 1027],
                                ln_scr[:, :], p_rbc[:, 0:1027], ALU.mult), "dv")
                    ln_done = C["dv"]

                lnstk.close()
                with ExitStack() as pstk:
                    wbuf = [pstk.enter_context(nc.sbuf_tensor(f"wh{i}_{sec}", [128, 32 * 128], BF16))
                            for i in range(2)]
                    convacc = [pstk.enter_context(nc.sbuf_tensor(f"ca{i}_{sec}", [128, 1024], F32))
                               for i in range(2)]
                    xsilu = pstk.enter_context(nc.sbuf_tensor(f"xsilu{sec}", [128, 1024], BF16))
                    xdd = pstk.enter_context(nc.sbuf_tensor(f"xdd{sec}", [128, 1024], BF16))
                    scr1 = pstk.enter_context(nc.sbuf_tensor(f"scr1{sec}", [128, 128], F32))
                    scr2 = pstk.enter_context(nc.sbuf_tensor(f"scr2{sec}", [128, 128], F32))
                    dt_fm = pstk.enter_context(nc.sbuf_tensor(f"dt_fm{sec}", [128, 1024], F32))
                    cs_loc = pstk.enter_context(nc.sbuf_tensor(f"cs_loc{sec}", [128, 128], F32))
                    srelu = convacc[0]
                    sabs = convacc[1]
                    psA = pstk.enter_context(nc.psum_tensor(f"psA{sec}", [128, 1536], F32))
                    psB = pstk.enter_context(nc.psum_tensor(f"psB{sec}", [128, 1536], F32))
                    psT = pstk.enter_context(nc.psum_tensor(f"psT{sec}", [128, 256], BF16))
                    psS = pstk.enter_context(nc.psum_tensor(f"psS{sec}", [128, 512], F32))

                    tiles = [0] + list(range(1, 9)) + \
                            (list(range(9, 17)) if own else []) + list(range(17, 81))
                    NT = len(tiles)
                    w_loaded = [0] * NT
                    proj_done = [0] * NT
                    psum_freed = [None] * NT

                    def sc_conv_ci(sc):
                        if sc <= 8:
                            return 64 + (sc - 1)
                        if sc <= 16:
                            return 72 + (sc - 9)
                        return sc - 17

                    for idx in range(NT):
                        sc = tiles[idx]
                        @block.gpsimd
                        def _(e, idx=idx, sc=sc):
                            if idx >= 2:
                                e.wait_ge(SEM["pe"], proj_done[idx - 2])
                            else:
                                e.wait_ge(SEM["dv"], ln_done)
                            e.wait_ge(SEM["wq"], C["wq"])
                            w_loaded[idx] = I(e.dma_start(
                                out=wbuf[idx % 2][:, :],
                                in_=w_hbc[sc].rearrange("p a b -> p (a b)")), "wq", 16)

                        @block.tensor
                        def _(e, idx=idx, sc=sc):
                            ps = psA if idx % 2 == 0 else psB
                            e.wait_ge(SEM["wq"], w_loaded[idx])
                            if idx == 0:
                                e.wait_ge(SEM["dv"], ln_done)
                            if idx >= 2:
                                s, v = psum_freed[idx - 2]
                                e.wait_ge(SEM[s], v)
                            for k in range(32):
                                for (o, w) in [(0, 512), (512, 512), (1024, 3)]:
                                    mm = e.matmul(ps[:, o:o + w],
                                                  wbuf[idx % 2][:, k * 128:(k + 1) * 128],
                                                  ln1[:, k * 1027 + o:k * 1027 + o + w],
                                                  start=(k == 0), stop=(k == 31))
                            proj_done[idx] = I(mm, "pe")

                        ps = psA if idx % 2 == 0 else psB
                        if sc == 0:
                            @block.scalar
                            def _(e, ps=ps):
                                e.wait_ge(SEM["pe"], proj_done[0])
                                e.wait_ge(SEM["ac"], silu_prev[0])
                                e.wait_ge(SEM["pe"], dt_chain_pe[0])
                                e.wait_ge(SEM["dv"], dt_chain_dv[0])
                                I(e.activation(srelu[:, :], ps[:, 3:1027], AF.Relu,
                                               bias=dtb_s[:, 0:1]), "ac")
                                I(e.activation(sabs[:, :], ps[:, 3:1027], AF.Abs,
                                               bias=dtb_s[:, 0:1]), "ac")
                                I(e.activation(sabs[:, :], sabs[:, :], AF.Exp, scale=-1.0), "ac")
                                I(e.activation(sabs[:, :], sabs[:, :], AF.Ln,
                                               bias=one_col[:, 0:1]), "ac")
                            dt_acts = C["ac"]
                            psum_freed[0] = ("ac", dt_acts)
                            @block.vector
                            def _(e):
                                e.wait_ge(SEM["ac"], dt_acts)
                                I(e.tensor_tensor(dt_fm[:, :], srelu[:, :], sabs[:, :], ALU.add), "dv")
                                I(e.tensor_scalar(srelu[:, :], dt_fm[:, :], expA_s[:, 0:1],
                                                  -1.0, ALU.mult, ALU.mult), "dv")
                            adt_done = C["dv"]
                            for c in range(8):
                                @block.tensor
                                def _(e, c=c):
                                    if c == 0:
                                        e.wait_ge(SEM["dv"], adt_done)
                                    s, v = psS_free[0]
                                    e.wait_ge(SEM[s], v)
                                    e.transpose(psS[:, 0:128], srelu[:, c * 128:(c + 1) * 128],
                                                idf_s[:, :])
                                    mm = e.transpose(psS[:, 128:256],
                                                     dt_fm[:, c * 128:(c + 1) * 128], idf_s[:, :])
                                    psS_ready[0] = I(mm, "pe")
                                @block.vector
                                def _(e, c=c):
                                    e.wait_ge(SEM["pe"], psS_ready[0])
                                    I(e.tensor_copy(A_dt_tm[:, c * 128:(c + 1) * 128], psS[:, 0:128]), "dv")
                                    I(e.tensor_copy(dt_tm[:, c * 128:(c + 1) * 128], psS[:, 128:256]), "dv")
                                evA = C["dv"]
                                @block.tensor
                                def _(e, c=c):
                                    e.wait_ge(SEM["dv"], evA)
                                    mm = e.matmul(psS[:, 256:384], ut_s[:, :],
                                                  A_dt_tm[:, c * 128:(c + 1) * 128],
                                                  start=True, stop=True)
                                    psS_ready[0] = I(mm, "pe")
                                @block.vector
                                def _(e, c=c):
                                    e.wait_ge(SEM["pe"], psS_ready[0])
                                    I(e.tensor_copy(cs_loc[:, :], psS[:, 256:384]), "dv")
                                evcs = C["dv"]
                                @block.tensor
                                def _(e, c=c):
                                    e.wait_ge(SEM["dv"], evcs)
                                    mm = e.transpose(psS[:, 384:512],
                                                     cs_loc[:, :], idf_s[:, :])
                                    psS_ready[0] = I(mm, "pe")
                                @block.vector
                                def _(e, c=c):
                                    e.wait_ge(SEM["pe"], psS_ready[0])
                                    I(e.tensor_copy(cslast[:, c:c + 1], psS[:, 511:512]), "dv")
                                evlast = C["dv"]
                                @block.scalar
                                def _(e, c=c):
                                    e.wait_ge(SEM["dv"], evlast)
                                    I(e.activation(scr1[:, :], psS[:, 384:512], AF.Exp,
                                                   bias=cslast[:, c:c + 1], scale=-1.0), "ac")
                                    I(e.activation(expcs_tm[:, c * 128:(c + 1) * 128],
                                                   cs_loc[:, :], AF.Exp), "ac")
                                    I(e.activation(gcol[:, :], cslast[:, c:c + 1], AF.Exp), "ac")
                                dec_done = C["ac"]
                                @block.vector
                                def _(e, c=c, sec=sec, own=own):
                                    e.wait_ge(SEM["ac"], dec_done)
                                    I(e.tensor_tensor(scr2[:, :], scr1[:, :],
                                                      dt_fm[:, c * 128:(c + 1) * 128], ALU.mult), "dv")
                                    if not own:
                                        I(e.tensor_scalar(
                                            scr2[:, :], scr2[:, :],
                                            valid_s[:, sec * 8 + c:sec * 8 + c + 1],
                                            None, ALU.mult), "dv")
                                dtdec_done = C["dv"]
                                @block.tensor
                                def _(e, c=c):
                                    e.wait_ge(SEM["dv"], dtdec_done)
                                    e.transpose(psS[:, 0:128], scr2[:, :], idf_s[:, :])
                                    mm = e.transpose(psS[0:1, 128:256], gcol[:, :], idf_s[:, :])
                                    psS_ready[0] = I(mm, "pe")
                                @block.vector
                                def _(e, c=c):
                                    e.wait_ge(SEM["pe"], psS_ready[0])
                                    I(e.tensor_copy(dtdecay_tm[:, c * 128:(c + 1) * 128],
                                                    psS[:, 0:128]), "dv")
                                    I(e.tensor_copy(grow[:, :], psS[0:1, 128:256]), "dv")
                                evdd = C["dv"]
                                @block.tensor
                                def _(e, c=c):
                                    e.wait_ge(SEM["dv"], evdd)
                                    mm = e.matmul(psS[:, 256:384], onesf_s[0:1, :], grow[:, :],
                                                  start=True, stop=True)
                                    psS_ready[0] = I(mm, "pe")
                                @block.vector
                                def _(e, c=c):
                                    e.wait_ge(SEM["pe"], psS_ready[0])
                                    I(e.tensor_copy(gamma_bc[:, c * 128:(c + 1) * 128],
                                                    psS[:, 256:384]), "dv")
                                    psS_free[0] = ["dv", C["dv"]]
                            dt_chain_pe[0] = C["pe"]
                            dt_chain_dv[0] = C["dv"]
                        else:
                            ci = sc_conv_ci(sc)
                            if sc <= 8:
                                dest, dbase = B_fm, (sc - 1) * 1024
                            elif sc <= 16:
                                dest, dbase = C_fm, (sc - 9) * 1024
                            else:
                                dest, dbase = xsilu, 0
                            tl = [0]
                            @block.vector
                            def _(e, idx=idx, ci=ci, ps=ps):
                                e.wait_ge(SEM["pe"], proj_done[idx])
                                e.wait_ge(SEM["ac"], silu_prev[0])
                                if idx == 1:
                                    e.wait_ge(SEM["pe"], dt_chain_pe[0])
                                    e.wait_ge(SEM["dv"], dt_chain_dv[0])
                                I(e.tensor_scalar(convacc[0][:, :], ps[:, 0:1024],
                                                  cw_s[:, 4 * ci:4 * ci + 1], None, ALU.mult), "dv")
                                I(e.scalar_tensor_tensor(convacc[1][:, :], ps[:, 1:1025],
                                                         cw_s[:, 4 * ci + 1:4 * ci + 2],
                                                         convacc[0][:, :], ALU.mult, ALU.add), "dv")
                                I(e.scalar_tensor_tensor(convacc[0][:, :], ps[:, 2:1026],
                                                         cw_s[:, 4 * ci + 2:4 * ci + 3],
                                                         convacc[1][:, :], ALU.mult, ALU.add), "dv")
                                tl[0] = I(e.scalar_tensor_tensor(convacc[1][:, :], ps[:, 3:1027],
                                                                 cw_s[:, 4 * ci + 3:4 * ci + 4],
                                                                 convacc[0][:, :], ALU.mult, ALU.add), "dv")
                            psum_freed[idx] = ("dv", tl[0])
                            @block.scalar
                            def _(e, sc=sc, ci=ci, dest=dest, dbase=dbase):
                                e.wait_ge(SEM["dv"], tl[0])
                                if sc >= 17:
                                    e.wait_ge(SEM["pe"], xs_used[0])
                                I(e.activation(dest[:, dbase:dbase + 1024], convacc[1][:, :],
                                               AF.Silu, bias=cb_s[:, ci:ci + 1]), "ac")
                                silu_prev[0] = C["ac"]
                            silu_done = silu_prev[0]

                            if sc <= 8:
                                g = sc - 1
                                for c in range(8):
                                    @block.tensor
                                    def _(e, g=g, c=c):
                                        e.wait_ge(SEM["ac"], silu_done)
                                        s, v = psT_free[0]
                                        e.wait_ge(SEM[s], v)
                                        mm = e.transpose(
                                            psT[:, (c % 2) * 128:(c % 2) * 128 + 128],
                                            B_fm[:, g * 1024 + c * 128:g * 1024 + (c + 1) * 128],
                                            idb_s[:, :])
                                        psS_ready[0] = I(mm, "pe")
                                    @block.vector
                                    def _(e, g=g, c=c):
                                        e.wait_ge(SEM["pe"], psS_ready[0])
                                        I(e.tensor_copy(
                                            B_tm[:, (g * 8 + c) * 128:(g * 8 + c + 1) * 128],
                                            psT[:, (c % 2) * 128:(c % 2) * 128 + 128]), "dv")
                                        psT_free[0] = ["dv", C["dv"]]
                            if sc >= 17:
                                xc = sc - 17
                                blk, wc = xc // 8, xc % 8
                                for c in range(8):
                                    @block.tensor
                                    def _(e, c=c):
                                        e.wait_ge(SEM["ac"], silu_done)
                                        s, v = psT_free[0]
                                        e.wait_ge(SEM[s], v)
                                        mm = e.transpose(
                                            psT[:, (c % 2) * 128:(c % 2) * 128 + 128],
                                            xsilu[:, c * 128:(c + 1) * 128], idb_s[:, :])
                                        psS_ready[0] = I(mm, "pe")
                                        if c == 7:
                                            xs_used[0] = C["pe"]
                                    @block.vector
                                    def _(e, c=c, wc=wc, own=own):
                                        e.wait_ge(SEM["pe"], psS_ready[0])
                                        if own and wc == 0 and c == 0:
                                            e.wait_ge(SEM["xq"], xblk_dma[0])
                                        I(e.tensor_copy(
                                            x_blk[:, c * 1024 + wc * 128:c * 1024 + wc * 128 + 128],
                                            psT[:, (c % 2) * 128:(c % 2) * 128 + 128]), "dv")
                                        psT_free[0] = ["dv", C["dv"]]

                                if wc == 7 and own:
                                    blk_ev = psT_free[0][1]
                                    @block.sync
                                    def _(e, blk=blk):
                                        e.wait_ge(SEM["dv"], blk_ev)
                                        e.wait_ge(SEM["xq"], C["xq"])
                                        xblk_dma[0] = I(e.dma_start(out=x_ssm[blk],
                                                                    in_=x_blk[:, :]), "xq", 16)
                                if wc == 7 and not own:
                                    for c in range(8):
                                        @block.vector
                                        def _(e, blk=blk, c=c):
                                            e.wait_ge(SEM["pe"], psS_ready[0])
                                            I(e.tensor_tensor(
                                                xdd[:, :].rearrange("p (h r) -> p h r", r=64),
                                                x_blk[:, c * 1024:(c + 1) * 1024]
                                                    .rearrange("p (h r) -> p h r", r=64),
                                                dtdecay_tm[:, c * 128 + 16 * blk:c * 128 + 16 * blk + 16]
                                                    .unsqueeze(2).broadcast_to([128, 16, 64]),
                                                ALU.mult), "dv")
                                        xdd_done = C["dv"]
                                        @block.tensor
                                        def _(e, blk=blk, c=c):
                                            e.wait_ge(SEM["dv"], xdd_done)
                                            s, v = psS_free[0]
                                            e.wait_ge(SEM[s], v)
                                            mm = e.matmul(
                                                psS[:, 0:512],
                                                B_tm[:, (blk * 8 + c) * 128:(blk * 8 + c + 1) * 128],
                                                xdd[:, 0:512], start=True, stop=True)
                                            psS_ready[0] = I(mm, "pe")
                                        @block.vector
                                        def _(e, blk=blk, c=c):
                                            e.wait_ge(SEM["pe"], psS_ready[0])
                                            for hh in range(8):
                                                h = 16 * blk + hh
                                                I(e.scalar_tensor_tensor(
                                                    R_bf[:, h * 64:(h + 1) * 64],
                                                    R_bf[:, h * 64:(h + 1) * 64],
                                                    gamma_bc[:, c * 128 + h:c * 128 + h + 1],
                                                    psS[:, hh * 64:(hh + 1) * 64],
                                                    ALU.mult, ALU.add), "dv")
                                            psS_free[0] = ["dv", C["dv"]]
                                        @block.tensor
                                        def _(e, blk=blk, c=c):
                                            s, v = psS_free[0]
                                            e.wait_ge(SEM[s], v)
                                            mm = e.matmul(
                                                psS[:, 0:512],
                                                B_tm[:, (blk * 8 + c) * 128:(blk * 8 + c + 1) * 128],
                                                xdd[:, 512:1024], start=True, stop=True)
                                            psS_ready[0] = I(mm, "pe")
                                        @block.vector
                                        def _(e, blk=blk, c=c):
                                            e.wait_ge(SEM["pe"], psS_ready[0])
                                            for hh in range(8, 16):
                                                h = 16 * blk + hh
                                                I(e.scalar_tensor_tensor(
                                                    R_bf[:, h * 64:(h + 1) * 64],
                                                    R_bf[:, h * 64:(h + 1) * 64],
                                                    gamma_bc[:, c * 128 + h:c * 128 + h + 1],
                                                    psS[:, (hh - 8) * 64:(hh - 7) * 64],
                                                    ALU.mult, ALU.add), "dv")
                                            psS_free[0] = ["dv", C["dv"]]
                sec_end[0], sec_end[1], sec_end[2] = C["pe"], C["dv"], C["ac"]
            barrier()

        with ExitStack() as phB:
            wgA = phB.enter_context(nc.sbuf_tensor("wgA", [128, 16 * 512], BF16))
            wgB = phB.enter_context(nc.sbuf_tensor("wgB", [128, 16 * 512], BF16))
            gbuf = [phB.enter_context(nc.sbuf_tensor(f"gb{i}", [128, 512], BF16))
                    for i in range(4)]
            pg = [phB.enter_context(nc.psum_tensor(f"pg{i}", [128, 512], F32))
                  for i in range(8)]
            wgA_ld = [0] * 16
            wgB_ld = [0] * 16
            wgA_used = [0] * 16
            wgB_used = [0] * 16
            gev = {}
            gdma = {}
            for j in range(16):
                @block.gpsimd
                def _(e, j=j):
                    if j >= 1:
                        e.wait_ge(SEM["pe"], wgA_used[j - 1])
                    e.wait_ge(SEM["wq"], C["wq"])
                    wgA_ld[j] = I(e.dma_start(
                        out=wgA[:, :],
                        in_=w_gate[j, 0]), "wq", 16)
                    if j >= 1:
                        e.wait_ge(SEM["pe"], wgB_used[j - 1])
                    e.wait_ge(SEM["wq"], C["wq"])
                    wgB_ld[j] = I(e.dma_start(
                        out=wgB[:, :],
                        in_=w_gate[j, 1]), "wq", 16)
                @block.tensor
                def _(e, j=j):
                    e.wait_ge(SEM["wq"], wgA_ld[j])
                    for t in range(8):
                        if (j - 1, t) in gev:
                            e.wait_ge(SEM["ac"], gev[(j - 1, t)])
                    for k in range(32):
                        if k == 16:
                            e.wait_ge(SEM["wq"], wgB_ld[j])
                        wg = wgA if k < 16 else wgB
                        kk = k % 16
                        for t in range(8):
                            mm = e.matmul(
                                pg[t][:, :],
                                ln1[:, k * 1027 + 3 + t * 128:k * 1027 + 3 + (t + 1) * 128],
                                wg[:, kk * 512:(kk + 1) * 512],
                                start=(k == 0), stop=(k == 31))
                        if k == 15:
                            wgA_used[j] = I(mm, "pe")
                        if k == 31:
                            wgB_used[j] = I(mm, "pe")
                mm_done = C["pe"]
                for t in range(8):
                    @block.scalar
                    def _(e, j=j, t=t):
                        e.wait_ge(SEM["pe"], mm_done)
                        if (j - 1, t) in gdma:
                            e.wait_ge(SEM["sq"], gdma[(j - 1, t)])
                        if (j, t - 4) in gdma:
                            e.wait_ge(SEM["sq"], gdma[(j, t - 4)])
                        gev[(j, t)] = I(e.activation(gbuf[t % 4][:, :], pg[t][:, :],
                                                     AF.Silu), "ac")
                        e.wait_ge(SEM["sq"], C["sq"])
                        gdma[(j, t)] = I(e.dma_start(
                            out=gate_d[t * 128:(t + 1) * 128, j * 512:(j + 1) * 512],
                            in_=gbuf[t % 4][:, :]), "sq", 16)
            barrier()

        scope_ln.close()

        with ExitStack() as phC:
            x_bk = [x_blk,
                    phC.enter_context(nc.sbuf_tensor("x_blk2", [128, 8192], BF16))]
            gate_bk = phC.enter_context(nc.sbuf_tensor("gate_bk", [128, 8192], BF16))
            y_tm = phC.enter_context(nc.sbuf_tensor("y_tm", [128, 8192], BF16))
            y_fmb = phC.enter_context(nc.sbuf_tensor("y_fmb", [128, 8192], BF16))
            rhsM = phC.enter_context(nc.sbuf_tensor("rhsM", [128, 2048], BF16))
            xddC = phC.enter_context(nc.sbuf_tensor("xddC", [128, 1024], BF16))
            e_sb = phC.enter_context(nc.sbuf_tensor("e_sb", [128, 256], F32))
            m_bf = phC.enter_context(nc.sbuf_tensor("m_bf", [128, 256], BF16))
            Gm_blk = phC.enter_context(nc.sbuf_tensor("Gm_blk", [128, 1024], BF16))
            tmp1 = phC.enter_context(nc.sbuf_tensor("tmp1", [128, 512], F32))
            tmp2 = phC.enter_context(nc.sbuf_tensor("tmp2", [128, 512], F32))
            sqS = phC.enter_context(nc.sbuf_tensor("sqS", [128, 1024], F32))
            redc = phC.enter_context(nc.sbuf_tensor("redc", [128, 1], F32))

            P1 = phC.enter_context(nc.psum_tensor("P1", [128, 1024], F32))
            P2 = phC.enter_context(nc.psum_tensor("P2", [128, 1024], F32))
            P3 = phC.enter_context(nc.psum_tensor("P3", [128, 1024], F32))
            P4 = phC.enter_context(nc.psum_tensor("P4", [128, 512], F32))
            P5 = phC.enter_context(nc.psum_tensor("P5", [128, 256], BF16))

            xld = [0, 0]
            gld = [0]
            y_wait_pe = [0]
            for blk in range(NBLK):
                xb = x_bk[blk % 2]
                @block.sync
                def _(e, blk=blk, xb=xb):
                    if blk >= 2:
                        for s, v in x_used[blk - 2]:
                            e.wait_ge(SEM[s], v)
                    e.wait_ge(SEM["xq"], C["xq"])
                    xld[blk % 2] = I(e.dma_start(out=xb[:, :], in_=x_ssm[blk]), "xq", 16)
                    if blk >= 1:
                        e.wait_ge(SEM["gp"], gate_used[0])
                    for t in range(8):
                        e.wait_ge(SEM["xq"], C["xq"])
                        gld[0] = I(e.dma_start(
                            out=gate_bk[:, t * 1024:(t + 1) * 1024],
                            in_=gate_d[t * 128:(t + 1) * 128,
                                       blk * 1024:(blk + 1) * 1024]), "xq", 16)
                for c in range(8):
                    @block.tensor
                    def _(e, blk=blk, c=c):
                        e.wait_ge(SEM["dv"], p4_free[0])
                        mm = e.matmul(
                            P4[:, 0:128],
                            B_fm[:, blk * 1024 + c * 128:blk * 1024 + (c + 1) * 128],
                            C_fm[:, blk * 1024 + c * 128:blk * 1024 + (c + 1) * 128],
                            start=True, stop=True)
                        gm_pe[0] = I(mm, "pe")
                    @block.vector
                    def _(e, blk=blk, c=c):
                        e.wait_ge(SEM["pe"], gm_pe[0])
                        I(e.tensor_tensor(Gm_blk[:, c * 128:(c + 1) * 128],
                                          P4[:, 0:128], ut_s[:, :], ALU.mult), "dv")
                        p4_free[0] = C["dv"]

                for c in range(8):
                    @block.vector
                    def _(e, blk=blk, c=c):
                        if c == 0:
                            e.wait_ge(SEM["xq"], xld[blk % 2])
                        e.wait_ge(SEM["pe"], diff_used[0])
                        I(e.tensor_tensor(
                            rhsM[:, :].rearrange("p (h l) -> p h l", l=128),
                            A_dt_tm[:, c * 128 + 16 * blk:c * 128 + 16 * blk + 16]
                                .unsqueeze(2).broadcast_to([128, 16, 128]),
                            ut_s[:, :].unsqueeze(1).broadcast_to([128, 16, 128]),
                            ALU.mult), "dv")
                    rhsM_done = C["dv"]
                    @block.tensor
                    def _(e, blk=blk, c=c):
                        e.wait_ge(SEM["dv"], rhsM_done)
                        e.wait_ge(SEM["ac"], exp_done[0])
                        e.wait_ge(SEM["dv"], s_used[0])
                        e.matmul(P1[:, 0:512], ind_s[:, :], rhsM[:, 0:512],
                                 start=True, stop=True)
                        e.matmul(P1[:, 512:1024], ind_s[:, :], rhsM[:, 512:1024],
                                 start=True, stop=True)
                        e.matmul(P2[:, 0:512], ind_s[:, :], rhsM[:, 1024:1536],
                                 start=True, stop=True)
                        mm = e.matmul(P2[:, 512:1024], ind_s[:, :], rhsM[:, 1536:2048],
                                      start=True, stop=True)
                        diff_pe[0] = I(mm, "pe")
                        diff_used[0] = diff_pe[0]
                    @block.tensor
                    def _(e, blk=blk, c=c):
                        e.wait_ge(SEM["dv"], yoff_evd[0])
                        e.wait_ge(SEM["dv"], rbf_ready[0])
                        e.matmul(P3[:, 0:512],
                                 C_fm[:, blk * 1024 + c * 128:blk * 1024 + (c + 1) * 128],
                                 R_bf[:, blk * 1024:blk * 1024 + 512],
                                 start=True, stop=True)
                        mm = e.matmul(P3[:, 512:1024],
                                      C_fm[:, blk * 1024 + c * 128:blk * 1024 + (c + 1) * 128],
                                      R_bf[:, blk * 1024 + 512:(blk + 1) * 1024],
                                      start=True, stop=True)
                        yoff_pe[0] = I(mm, "pe")
                    for halfh in range(2):
                        for hh in range(halfh * 8, halfh * 8 + 8):
                            src = P1 if hh < 8 else P2
                            par = hh % 2
                            @block.scalar
                            def _(e, c=c, hh=hh, src=src, par=par):
                                e.wait_ge(SEM["pe"], diff_pe[0])
                                e.wait_ge(SEM["dv"], mprime_done.get(("e", par), 0))
                                exp_ac[(c, hh)] = I(e.activation(
                                    e_sb[:, par * 128:(par + 1) * 128],
                                    src[:, (hh % 8) * 128:(hh % 8 + 1) * 128],
                                    AF.Exp), "ac")
                                if hh == 15:
                                    exp_done[0] = C["ac"]
                            @block.vector
                            def _(e, blk=blk, c=c, hh=hh, par=par):
                                e.wait_ge(SEM["ac"], exp_ac[(c, hh)])
                                e.wait_ge(SEM["pe"], ydiag_used.get(par, 0))
                                hcol = c * 128 + 16 * blk + hh
                                I(e.scalar_tensor_tensor(
                                    m_bf[:, par * 128:(par + 1) * 128],
                                    e_sb[:, par * 128:(par + 1) * 128],
                                    dt_tm[:, hcol:hcol + 1],
                                    Gm_blk[:, c * 128:(c + 1) * 128],
                                    ALU.mult, ALU.mult), "dv")
                                mprime_done[("e", par)] = C["dv"]
                                mprime_done[(c, hh)] = C["dv"]
                            @block.tensor
                            def _(e, blk=blk, c=c, hh=hh, par=par, xb=xb):
                                e.wait_ge(SEM["dv"], mprime_done[(c, hh)])
                                if hh % 8 == 0:
                                    e.wait_ge(SEM["dv"], p4_free[0])
                                mm = e.matmul(
                                    P4[:, (hh % 8) * 64:(hh % 8 + 1) * 64],
                                    m_bf[:, par * 128:(par + 1) * 128],
                                    xb[:, c * 1024 + hh * 64:c * 1024 + (hh + 1) * 64],
                                    start=True, stop=True)
                                ydiag_used[par] = I(mm, "pe")
                                if hh % 8 == 7:
                                    ydiag_pe[0] = C["pe"]
                        @block.vector
                        def _(e, blk=blk, c=c, halfh=halfh, xb=xb):
                            e.wait_ge(SEM["pe"], ydiag_pe[0])
                            e.wait_ge(SEM["pe"], yoff_pe[0])
                            if c == 0 and halfh == 0:
                                e.wait_ge(SEM["pe"], y_wait_pe[0])
                            hb = 16 * blk + halfh * 8
                            I(e.tensor_tensor(
                                tmp1[:, :].rearrange("p (h r) -> p h r", r=64),
                                P3[:, halfh * 512:(halfh + 1) * 512]
                                    .rearrange("p (h r) -> p h r", r=64),
                                expcs_tm[:, c * 128 + hb:c * 128 + hb + 8]
                                    .unsqueeze(2).broadcast_to([128, 8, 64]),
                                ALU.mult), "dv")
                            I(e.tensor_tensor(tmp2[:, :], tmp1[:, :], P4[:, :],
                                              ALU.add), "dv")
                            p4_free[0] = C["dv"]
                            I(e.tensor_tensor(
                                tmp1[:, :].rearrange("p (h r) -> p h r", r=64),
                                xb[:, c * 1024 + halfh * 512:c * 1024 + (halfh + 1) * 512]
                                    .rearrange("p (h r) -> p h r", r=64),
                                dbc_s[:, hb:hb + 8].unsqueeze(2).broadcast_to([128, 8, 64]),
                                ALU.mult), "dv")
                            I(e.tensor_tensor(
                                y_tm[:, c * 1024 + halfh * 512:c * 1024 + (halfh + 1) * 512],
                                tmp2[:, :], tmp1[:, :], ALU.add), "dv")
                            if halfh == 1:
                                yoff_evd[0] = C["dv"]
                    @block.vector
                    def _(e, blk=blk, c=c, xb=xb):
                        e.wait_ge(SEM["pe"], s_pe[0])
                        I(e.tensor_tensor(
                            xddC[:, :].rearrange("p (h r) -> p h r", r=64),
                            xb[:, c * 1024:(c + 1) * 1024]
                                .rearrange("p (h r) -> p h r", r=64),
                            dtdecay_tm[:, c * 128 + 16 * blk:c * 128 + 16 * blk + 16]
                                .unsqueeze(2).broadcast_to([128, 16, 64]),
                            ALU.mult), "dv")
                    xddC_done = C["dv"]
                    @block.tensor
                    def _(e, blk=blk, c=c):
                        e.wait_ge(SEM["dv"], xddC_done)
                        e.wait_ge(SEM["ac"], exp_done[0])
                        e.matmul(P1[:, 0:512],
                                 B_tm[:, (blk * 8 + c) * 128:(blk * 8 + c + 1) * 128],
                                 xddC[:, 0:512], start=True, stop=True)
                        mm = e.matmul(P1[:, 512:1024],
                                      B_tm[:, (blk * 8 + c) * 128:(blk * 8 + c + 1) * 128],
                                      xddC[:, 512:1024], start=True, stop=True)
                        s_pe[0] = I(mm, "pe")
                    @block.vector
                    def _(e, blk=blk, c=c):
                        e.wait_ge(SEM["pe"], s_pe[0])
                        for hh in range(16):
                            h = 16 * blk + hh
                            I(e.scalar_tensor_tensor(
                                R_bf[:, h * 64:(h + 1) * 64],
                                R_bf[:, h * 64:(h + 1) * 64],
                                gamma_bc[:, c * 128 + h:c * 128 + h + 1],
                                P1[:, hh * 64:(hh + 1) * 64],
                                ALU.mult, ALU.add), "dv")
                        s_used[0] = C["dv"]
                        rbf_ready[0] = C["dv"]
                    @block.scalar
                    def _(e, blk=blk, c=c):
                        e.wait_ge(SEM["dv"], yoff_evd[0])
                        I(e.activation(sqS[:, :], y_tm[:, c * 1024:(c + 1) * 1024],
                                       AF.Square), "ac")
                    sq_ac = C["ac"]
                    @block.vector
                    def _(e, blk=blk, c=c):
                        e.wait_ge(SEM["ac"], sq_ac)
                        I(e.reduce_sum(redc[:, 0:1], sqS[:, :], axis=AX.X), "dv")
                        I(e.tensor_tensor(ssq[:, c:c + 1], ssq[:, c:c + 1],
                                          redc[:, 0:1], ALU.add), "dv")

                ytm_done = C["dv"]
                @block.gpsimd
                def _(e, blk=blk):
                    e.wait_ge(SEM["xq"], gld[0])
                    e.wait_ge(SEM["dv"], ytm_done)
                    for t in range(8):
                        I(e.tensor_tensor(y_tm[:, t * 1024:(t + 1) * 1024],
                                          y_tm[:, t * 1024:(t + 1) * 1024],
                                          gate_bk[:, t * 1024:(t + 1) * 1024],
                                          ALU.mult), "gp")
                    gate_used[0] = C["gp"]
                yg_done = C["gp"]
                for c in range(8):
                    for f in range(8):
                        @block.tensor
                        def _(e, blk=blk, c=c, f=f):
                            e.wait_ge(SEM["gp"], yg_done)
                            s, v = p5_free[0]
                            e.wait_ge(SEM[s], v)
                            mm = e.transpose(
                                P5[:, (f % 2) * 128:(f % 2) * 128 + 128],
                                y_tm[:, c * 1024 + f * 128:c * 1024 + (f + 1) * 128],
                                idb_s[:, :])
                            p5_pe[0] = I(mm, "pe")
                        @block.vector
                        def _(e, blk=blk, c=c, f=f):
                            e.wait_ge(SEM["pe"], p5_pe[0])
                            if c == 0 and f == 0:
                                e.wait_ge(SEM["sq"], yfm_dma[0])
                            k = blk * 8 + f
                            I(e.tensor_scalar(
                                y_fmb[:, f * 1024 + c * 128:f * 1024 + (c + 1) * 128],
                                P5[:, (f % 2) * 128:(f % 2) * 128 + 128],
                                nw_s[:, k:k + 1], None, ALU.mult), "dv")
                            p5_free[0] = ["dv", C["dv"]]
                y_wait_pe[0] = p5_pe[0]
                yfm_done = C["dv"]
                @block.scalar
                def _(e, blk=blk):
                    e.wait_ge(SEM["dv"], yfm_done)
                    for f in range(8):
                        e.wait_ge(SEM["sq"], C["sq"])
                        yfm_dma[0] = I(e.dma_start(
                            out=y_d[blk * 8 + f],
                            in_=y_fmb[:, f * 1024:(f + 1) * 1024]), "sq", 16)
                x_used[blk] = [("pe", C["pe"]), ("dv", C["dv"])]
            barrier()

        scope_ssm.close()

        with ExitStack() as phD1:
            y_fm = phD1.enter_context(nc.sbuf_tensor("y_fm", [128, 64 * 1024], BF16))
            wob = [phD1.enter_context(nc.sbuf_tensor(f"wob{i}", [128, 512], BF16))
                   for i in range(8)]
            xsl = [phD1.enter_context(nc.sbuf_tensor(f"xsl{i}", [128, 512], F32))
                   for i in range(8)]
            r1b = [phD1.enter_context(nc.sbuf_tensor(f"r1b{i}", [128, 512], BF16))
                   for i in range(8)]
            po = [phD1.enter_context(nc.psum_tensor(f"po{i}", [128, 512], F32))
                  for i in range(8)]

            @block.vector
            def _(e):
                I(e.tensor_scalar(rstd[:, :], ssq[:, :], 1.0 / 8192.0, EPS,
                                  ALU.mult, ALU.add), "dv")
            v1 = C["dv"]
            @block.scalar
            def _(e):
                e.wait_ge(SEM["dv"], v1)
                I(e.activation(rstd[:, :], rstd[:, :], AF.Sqrt), "ac")
            v2 = C["ac"]
            @block.vector
            def _(e):
                e.wait_ge(SEM["ac"], v2)
                I(e.reciprocal(rstd[:, :], rstd[:, :]), "dv")
            rstd_done = C["dv"]

            yfm_ld = [0]
            @block.sync
            def _(e):
                for i in range(8):
                    e.wait_ge(SEM["xq"], C["xq"])
                    yfm_ld[0] = I(e.dma_start(
                        out=y_fm[:, i * 8192:(i + 1) * 8192]
                            .rearrange("p (k t) -> p k t", t=1024),
                        in_=y_d[i * 8:(i + 1) * 8].rearrange("k p t -> p k t")), "xq", 16)

            wo_ld = {}
            wo_used = {}
            ev_done = {}
            xsl_ld = {}
            r1_dma = {}
            for j in range(8):
                @block.sync
                def _(e, j=j):
                    for t in range(8):
                        if (j - 1, t) in ev_done:
                            e.wait_ge(SEM["dv"], ev_done[(j - 1, t)])
                        e.wait_ge(SEM["xq"], C["xq"])
                        xsl_ld[(j, t)] = I(e.dma_start(
                            out=xsl[t][:, :],
                            in_=x_tm_d[t * 128:(t + 1) * 128,
                                       j * 512:(j + 1) * 512]), "xq", 16)
                for k in range(64):
                    @block.gpsimd
                    def _(e, j=j, k=k):
                        prev = (j * 64 + k) - 8
                        if prev >= 0:
                            e.wait_ge(SEM["pe"], wo_used[prev])
                        e.wait_ge(SEM["wq"], C["wq"])
                        wo_ld[(j, k)] = I(e.dma_start(out=wob[k % 8][:, :],
                                                      in_=w_out[j, k]), "wq", 16)
                    @block.tensor
                    def _(e, j=j, k=k):
                        if j == 0 and k == 0:
                            e.wait_ge(SEM["xq"], yfm_ld[0])
                        e.wait_ge(SEM["wq"], wo_ld[(j, k)])
                        if k == 0:
                            for t in range(8):
                                if (j - 1, t) in ev_done:
                                    e.wait_ge(SEM["dv"], ev_done[(j - 1, t)])
                        for t in range(8):
                            mm = e.matmul(
                                po[t][:, :],
                                y_fm[:, k * 1024 + t * 128:k * 1024 + (t + 1) * 128],
                                wob[k % 8][:, :],
                                start=(k == 0), stop=(k == 63))
                        wo_used[j * 64 + k] = I(mm, "pe")
                mm_all = C["pe"]
                for t in range(8):
                    @block.vector
                    def _(e, j=j, t=t):
                        e.wait_ge(SEM["pe"], mm_all)
                        e.wait_ge(SEM["dv"], rstd_done)
                        e.wait_ge(SEM["xq"], xsl_ld[(j, t)])
                        if (j - 1, t) in r1_dma:
                            e.wait_ge(SEM["sq"], r1_dma[(j - 1, t)])
                        ev_done[(j, t)] = I(e.scalar_tensor_tensor(
                            r1b[t][:, :], po[t][:, :], rstd[:, t:t + 1],
                            xsl[t][:, :], ALU.mult, ALU.add), "dv")
                    @block.scalar
                    def _(e, j=j, t=t):
                        e.wait_ge(SEM["dv"], ev_done[(j, t)])
                        e.wait_ge(SEM["sq"], C["sq"])
                        r1_dma[(j, t)] = I(e.dma_start(
                            out=r1_d[t * 128:(t + 1) * 128, j * 512:(j + 1) * 512],
                            in_=r1b[t][:, :]), "sq", 16)
            barrier()

        ln2_fm = top.enter_context(nc.sbuf_tensor("ln2_fm", [128, 32 * 1024], BF16))
        with ExitStack() as phD2:
            r1t = phD2.enter_context(nc.sbuf_tensor("r1t", [128, 4096], BF16))
            sq4 = phD2.enter_context(nc.sbuf_tensor("sq4", [128, 4096], F32))
            ln2t = phD2.enter_context(nc.sbuf_tensor("ln2t", [128, 4096], BF16))
            mcol = phD2.enter_context(nc.sbuf_tensor("mcol", [128, 1], F32))
            qcol = phD2.enter_context(nc.sbuf_tensor("qcol", [128, 1], F32))
            vcol = phD2.enter_context(nc.sbuf_tensor("vcol", [128, 1], F32))
            P5b = phD2.enter_context(nc.psum_tensor("P5b", [128, 256], BF16))
            p5_free[0] = ["dv", 0]
            p5_pe[0] = 0

            for t in range(8):
                @block.sync
                def _(e, t=t):
                    if t >= 1:
                        e.wait_ge(SEM["dv"], ln2t_done[0])
                    e.wait_ge(SEM["xq"], C["xq"])
                    r1t_ld[0] = I(e.dma_start(out=r1t[:, :],
                                              in_=r1_d[t * 128:(t + 1) * 128, :]), "xq", 16)
                @block.vector
                def _(e, t=t):
                    e.wait_ge(SEM["xq"], r1t_ld[0])
                    I(e.reduce_sum(mcol[:, 0:1], r1t[:, :], axis=AX.X), "dv")
                @block.scalar
                def _(e, t=t):
                    e.wait_ge(SEM["xq"], r1t_ld[0])
                    e.wait_ge(SEM["dv"], sq4_used[0])
                    I(e.activation(sq4[:, :], r1t[:, :], AF.Square), "ac")
                sq_d = C["ac"]
                @block.vector
                def _(e, t=t):
                    e.wait_ge(SEM["ac"], sq_d)
                    sq4_used[0] = I(e.reduce_sum(qcol[:, 0:1], sq4[:, :], axis=AX.X), "dv")
                    I(e.tensor_scalar(mcol[:, :], mcol[:, :], 1.0 / 4096.0, None,
                                      ALU.mult), "dv")
                    I(e.tensor_scalar(qcol[:, :], qcol[:, :], 1.0 / 4096.0, None,
                                      ALU.mult), "dv")
                    I(e.tensor_tensor(vcol[:, :], mcol[:, :], mcol[:, :], ALU.mult), "dv")
                    I(e.tensor_tensor(vcol[:, :], qcol[:, :], vcol[:, :], ALU.subtract), "dv")
                var_d = C["dv"]
                @block.scalar
                def _(e, t=t):
                    e.wait_ge(SEM["dv"], var_d)
                    I(e.activation(vcol[:, :], vcol[:, :], AF.Sqrt, bias=eps_col[:, 0:1]), "ac")
                sd_d = C["ac"]
                @block.vector
                def _(e, t=t):
                    e.wait_ge(SEM["ac"], sd_d)
                    I(e.reciprocal(vcol[:, :], vcol[:, :]), "dv")
                    e.wait_ge(SEM["pe"], p5_pe[0])
                    ln2t_done[0] = I(e.tensor_scalar(ln2t[:, :], r1t[:, :], mcol[:, 0:1],
                                                     vcol[:, 0:1], ALU.subtract,
                                                     ALU.mult), "dv")
                for d in range(32):
                    @block.tensor
                    def _(e, t=t, d=d):
                        e.wait_ge(SEM["dv"], ln2t_done[0])
                        s, v = p5_free[0]
                        e.wait_ge(SEM[s], v)
                        mm = e.transpose(P5b[:, (d % 2) * 128:(d % 2) * 128 + 128],
                                         ln2t[:, d * 128:(d + 1) * 128], idb_s[:, :])
                        p5_pe[0] = I(mm, "pe")
                    @block.vector
                    def _(e, t=t, d=d):
                        e.wait_ge(SEM["pe"], p5_pe[0])
                        I(e.tensor_copy(
                            ln2_fm[:, d * 1024 + t * 128:d * 1024 + (t + 1) * 128],
                            P5b[:, (d % 2) * 128:(d % 2) * 128 + 128]), "dv")
                        p5_free[0] = ["dv", C["dv"]]
            barrier()

        wr_s = top.enter_context(nc.sbuf_tensor("wr_s", [128, 32 * 8], BF16))
        wa_s = top.enter_context(nc.sbuf_tensor("wa_s", [128, 32 * 128], BF16))
        wB_s = top.enter_context(nc.sbuf_tensor("wB_s", [128, 4096], BF16))
        bb_s = top.enter_context(nc.sbuf_tensor("bb_s", [1, 4096], BF16))
        h_fm = top.enter_context(nc.sbuf_tensor("h_fm", [128, 8 * 128], BF16))
        with ExitStack() as phD3:
            bb_f = phD3.enter_context(nc.sbuf_tensor("bb_f", [1, 4096], F32))
            prob = phD3.enter_context(nc.sbuf_tensor("prob", [128, 8], F32))
            mx1 = phD3.enter_context(nc.sbuf_tensor("mx1", [128, 1], F32))
            mx2 = phD3.enter_context(nc.sbuf_tensor("mx2", [128, 1], F32))
            ge1 = phD3.enter_context(nc.sbuf_tensor("ge1", [128, 8], F32))
            ge2 = phD3.enter_context(nc.sbuf_tensor("ge2", [128, 8], F32))
            p2s = phD3.enter_context(nc.sbuf_tensor("p2s", [128, 8], F32))
            den = phD3.enter_context(nc.sbuf_tensor("den", [128, 1], F32))
            w1c = phD3.enter_context(nc.sbuf_tensor("w1c", [128, 1], F32))
            w2c = phD3.enter_context(nc.sbuf_tensor("w2c", [128, 1], F32))
            comb = phD3.enter_context(nc.sbuf_tensor("comb", [128, 8], F32))
            h_bf = phD3.enter_context(nc.sbuf_tensor("h_bf", [128, 128], BF16))
            pr = phD3.enter_context(nc.psum_tensor("pr", [128, 8], F32))
            ph = phD3.enter_context(nc.psum_tensor("ph", [128, 128], F32))
            P5c = phD3.enter_context(nc.psum_tensor("P5c", [128, 256], BF16))

            @block.gpsimd
            def _(e):
                e.wait_ge(SEM["wq"], C["wq"])
                I(e.dma_start(out=wr_s[:, :], in_=w_rout.rearrange("p a b -> p (a b)")),
                  "wq", 16)
                e.wait_ge(SEM["wq"], C["wq"])
                I(e.dma_start(out=wa_s[:, :], in_=w_A.rearrange("p a b -> p (a b)")),
                  "wq", 16)
                e.wait_ge(SEM["wq"], C["wq"])
                I(e.dma_start(out=wB_s[:, :], in_=w_B), "wq", 16)
            wsm_ld = C["wq"]
            @block.sync
            def _(e):
                e.wait_ge(SEM["xq"], C["xq"])
                I(e.dma_start(out=bb_f[:, :], in_=b_base_d), "xq", 16)
            bb_ld = C["xq"]
            @block.vector
            def _(e):
                e.wait_ge(SEM["xq"], bb_ld)
                I(e.tensor_copy(bb_s[:, :], bb_f[:, :]), "dv")
            bb_done = C["dv"]

            for t in range(8):
                @block.tensor
                def _(e, t=t):
                    e.wait_ge(SEM["wq"], wsm_ld)
                    if t >= 1:
                        e.wait_ge(SEM["dv"], hfm_ev[0])
                        e.wait_ge(SEM["ac"], sm2_prev[0])
                    for k in range(32):
                        e.matmul(pr[:, :],
                                 ln2_fm[:, k * 1024 + t * 128:k * 1024 + (t + 1) * 128],
                                 wr_s[:, k * 8:(k + 1) * 8],
                                 start=(k == 0), stop=(k == 31))
                        mm = e.matmul(ph[:, :],
                                      ln2_fm[:, k * 1024 + t * 128:k * 1024 + (t + 1) * 128],
                                      wa_s[:, k * 128:(k + 1) * 128],
                                      start=(k == 0), stop=(k == 31))
                    rt_pe[0] = I(mm, "pe")
                @block.vector
                def _(e, t=t):
                    e.wait_ge(SEM["pe"], rt_pe[0])
                    I(e.reduce_max(mx1[:, 0:1], pr[:, :], axis=AX.X), "dv")
                    I(e.tensor_scalar(mx1[:, :], mx1[:, :], -1.0, None, ALU.mult), "dv")
                sm1 = C["dv"]
                @block.scalar
                def _(e, t=t):
                    e.wait_ge(SEM["dv"], sm1)
                    sm2_prev[0] = I(e.activation(prob[:, :], pr[:, :], AF.Exp,
                                                 bias=mx1[:, 0:1]), "ac")
                sm2 = sm2_prev[0]
                @block.vector
                def _(e, t=t):
                    e.wait_ge(SEM["ac"], sm2)
                    I(e.reduce_sum(den[:, 0:1], prob[:, :], axis=AX.X), "dv")
                    I(e.reciprocal(w1c[:, 0:1], den[:, 0:1]), "dv")
                    I(e.tensor_scalar(prob[:, :], prob[:, :], w1c[:, 0:1], None,
                                      ALU.mult), "dv")
                    I(e.reduce_max(mx1[:, 0:1], prob[:, :], axis=AX.X), "dv")
                    I(e.tensor_scalar(ge1[:, :], prob[:, :], mx1[:, 0:1], None,
                                      ALU.is_ge), "dv")
                    I(e.scalar_tensor_tensor(p2s[:, :], ge1[:, :], -1e9, prob[:, :],
                                             ALU.mult, ALU.add), "dv")
                    I(e.reduce_max(mx2[:, 0:1], p2s[:, :], axis=AX.X), "dv")
                    I(e.tensor_scalar(ge2[:, :], p2s[:, :], mx2[:, 0:1], None,
                                      ALU.is_ge), "dv")
                    I(e.tensor_tensor(den[:, :], mx1[:, :], mx2[:, :], ALU.add), "dv")
                    I(e.reciprocal(den[:, 0:1], den[:, 0:1]), "dv")
                    I(e.tensor_scalar(w1c[:, :], mx1[:, :], den[:, 0:1], 2.0,
                                      ALU.mult, ALU.mult), "dv")
                    I(e.tensor_scalar(w2c[:, :], mx2[:, :], den[:, 0:1], 2.0,
                                      ALU.mult, ALU.mult), "dv")
                    I(e.tensor_scalar(comb[:, :], ge1[:, :], w1c[:, 0:1], None,
                                      ALU.mult), "dv")
                    I(e.scalar_tensor_tensor(comb[:, :], ge2[:, :], w2c[:, 0:1],
                                             comb[:, :], ALU.mult, ALU.add), "dv")
                    I(e.tensor_tensor(h_bf[:, :].rearrange("p (ex r) -> p ex r", r=16),
                                      ph[:, :].rearrange("p (ex r) -> p ex r", r=16),
                                      comb[:, :].unsqueeze(2).broadcast_to([128, 8, 16]),
                                      ALU.mult), "dv")
                hprime = C["dv"]
                @block.tensor
                def _(e, t=t):
                    e.wait_ge(SEM["dv"], hprime)
                    s, v = p5_free[0]
                    e.wait_ge(SEM[s], v)
                    mm = e.transpose(P5c[:, 0:128], h_bf[:, :], idb_s[:, :])
                    p5_pe[0] = I(mm, "pe")
                @block.vector
                def _(e, t=t):
                    e.wait_ge(SEM["pe"], p5_pe[0])
                    hfm_ev[0] = I(e.tensor_copy(h_fm[:, t * 128:(t + 1) * 128],
                                                P5c[:, 0:128]), "dv")
                    p5_free[0] = ["dv", hfm_ev[0]]
            barrier()

        with ExitStack() as phD4:
            wbb = [phD4.enter_context(nc.sbuf_tensor(f"wbb{i}", [128, 512], BF16))
                   for i in range(8)]
            r1sl = [phD4.enter_context(nc.sbuf_tensor(f"r1sl{i}", [128, 512], BF16))
                    for i in range(8)]
            obuf = [phD4.enter_context(nc.sbuf_tensor(f"obuf{i}", [128, 512], F32))
                    for i in range(8)]
            po4 = [phD4.enter_context(nc.psum_tensor(f"po4{i}", [128, 512], F32))
                   for i in range(8)]

            wb_ld = {}
            wb_used = {}
            oev = {}
            r1s_ld = {}
            out_dma = {}
            d4_pe = [0]
            for j in range(8):
                @block.sync
                def _(e, j=j):
                    for t in range(8):
                        if (j - 1, t) in oev:
                            e.wait_ge(SEM["dv"], oev[(j - 1, t)])
                        e.wait_ge(SEM["xq"], C["xq"])
                        r1s_ld[(j, t)] = I(e.dma_start(
                            out=r1sl[t][:, :],
                            in_=r1_d[t * 128:(t + 1) * 128,
                                     j * 512:(j + 1) * 512]), "xq", 16)
                for k in range(32):
                    @block.gpsimd
                    def _(e, j=j, k=k):
                        prev = (j * 32 + k) - 8
                        if prev >= 0:
                            e.wait_ge(SEM["pe"], wb_used[prev])
                        e.wait_ge(SEM["wq"], C["wq"])
                        wb_ld[(j, k)] = I(e.dma_start(out=wbb[k % 8][:, :],
                                                      in_=w_base[j, k]), "wq", 16)
                    @block.tensor
                    def _(e, j=j, k=k):
                        if j == 0 and k == 0:
                            e.wait_ge(SEM["dv"], bb_done)
                        e.wait_ge(SEM["wq"], wb_ld[(j, k)])
                        if k == 0:
                            for t in range(8):
                                if (j - 1, t) in oev:
                                    e.wait_ge(SEM["dv"], oev[(j - 1, t)])
                        for t in range(8):
                            mm = e.matmul(
                                po4[t][:, :],
                                ln2_fm[:, k * 1024 + t * 128:k * 1024 + (t + 1) * 128],
                                wbb[k % 8][:, :],
                                start=(k == 0), stop=False)
                        wb_used[j * 32 + k] = I(mm, "pe")
                @block.tensor
                def _(e, j=j):
                    for t in range(8):
                        e.matmul(po4[t][:, :], onesb_s[0:1, :],
                                 bb_s[:, j * 512:(j + 1) * 512],
                                 start=False, stop=False)
                        mm = e.matmul(po4[t][:, :], h_fm[:, t * 128:(t + 1) * 128],
                                      wB_s[:, j * 512:(j + 1) * 512],
                                      start=False, stop=True)
                    d4_pe[0] = I(mm, "pe")
                for t in range(8):
                    @block.vector
                    def _(e, j=j, t=t):
                        e.wait_ge(SEM["pe"], d4_pe[0])
                        e.wait_ge(SEM["xq"], r1s_ld[(j, t)])
                        if (j - 1, t) in out_dma:
                            e.wait_ge(SEM["sq"], out_dma[(j - 1, t)])
                        oev[(j, t)] = I(e.tensor_tensor(obuf[t][:, :], po4[t][:, :],
                                                        r1sl[t][:, :], ALU.add), "dv")
                    @block.scalar
                    def _(e, j=j, t=t):
                        e.wait_ge(SEM["dv"], oev[(j, t)])
                        e.wait_ge(SEM["sq"], C["sq"])
                        out_dma[(j, t)] = I(e.dma_start(
                            out=out_d[t * 128:(t + 1) * 128, j * 512:(j + 1) * 512],
                            in_=obuf[t][:, :]), "sq", 16)
            @block.scalar
            def _(e):
                e.wait_ge(SEM["sq"], C["sq"])
    return nc


def kernel(L0_attn_hidden_states, L0_attn_in_proj_w, L0_attn_conv_w, L0_attn_conv_b,
           L0_attn_dt_bias, L0_attn_A_log, L0_attn_D, L0_attn_norm_w,
           L0_attn_out_proj_w, L0_ffn_W_base, L0_ffn_b_base, L0_ffn_W_router,
           L0_ffn_A_stack, L0_ffn_B_stack):
    x = np.asarray(L0_attn_hidden_states, dtype=np.float32)
    WT = np.ascontiguousarray(np.asarray(L0_attn_in_proj_w, np.float32).T)

    order = [DTB] + [BB + 128 * g for g in range(8)] + \
            [CB + 128 * g for g in range(8)] + [XB + 128 * xc for xc in range(64)]
    w_hbc = np.empty((81, 128, 32, 128), np.float32)
    for i, cb in enumerate(order):
        w_hbc[i] = WT[:, cb:cb + 128].reshape(32, 128, 128).transpose(1, 0, 2)

    w_gate = np.ascontiguousarray(
        WT[:, 0:8192].reshape(32, 128, 16, 512).transpose(2, 1, 0, 3)
        .reshape(16, 128, 2, 16, 512).transpose(0, 2, 1, 3, 4)
        .reshape(16, 2, 128, 8192))
    OT = np.ascontiguousarray(np.asarray(L0_attn_out_proj_w, np.float32).T)
    w_out = np.ascontiguousarray(OT.reshape(64, 128, 8, 512).transpose(2, 0, 1, 3))
    BT = np.ascontiguousarray(np.asarray(L0_ffn_W_base, np.float32).T)
    w_base = np.ascontiguousarray(BT.reshape(32, 128, 8, 512).transpose(2, 0, 1, 3))
    w_rout = np.ascontiguousarray(
        np.asarray(L0_ffn_W_router, np.float32).T.reshape(32, 128, 8).transpose(1, 0, 2))
    A_cat = np.ascontiguousarray(
        np.asarray(L0_ffn_A_stack, np.float32).transpose(2, 0, 1).reshape(4096, 128))
    w_A = np.ascontiguousarray(A_cat.reshape(32, 128, 128).transpose(1, 0, 2))
    w_B = np.ascontiguousarray(
        np.asarray(L0_ffn_B_stack, np.float32).transpose(0, 2, 1).reshape(128, 4096))
    conv_w = np.ascontiguousarray(
        np.asarray(L0_attn_conv_w, np.float32).reshape(80, 128, 4).transpose(1, 0, 2))
    conv_b = np.ascontiguousarray(
        np.asarray(L0_attn_conv_b, np.float32).reshape(80, 128).T)
    dt_bias = np.asarray(L0_attn_dt_bias, np.float32).reshape(128, 1).copy()
    A_log = np.asarray(L0_attn_A_log, np.float32).reshape(128, 1).copy()
    D_bc = np.ascontiguousarray(
        np.broadcast_to(np.asarray(L0_attn_D, np.float32), (128, 128)))
    b_base = np.asarray(L0_ffn_b_base, np.float32).reshape(1, 4096).copy()
    norm_w = np.ascontiguousarray(
        np.asarray(L0_attn_norm_w, np.float32).reshape(64, 128).T)

    in_maps = []
    for core in range(8):
        b, q = core // 4, core % 4
        t0 = q * 1024
        xT = np.ascontiguousarray(x[b].T)
        xp = np.zeros((4096, 3 + 4096), np.float32)
        xp[:, 3 + 3072 - t0:3 + 3072 + 1024] = xT[:, 0:t0 + 1024]
        x_spans = np.empty((4, 32, 128, 1027), np.float32)
        for sec in range(4):
            x_spans[sec] = xp[:, sec * 1024:sec * 1024 + 1027] \
                .reshape(32, 128, 1027)
        x_tmc = np.ascontiguousarray(x[b, t0:t0 + 1024, :])
        valid = np.zeros((128, 24), np.float32)
        pad_chunks = (3 - q) * 8
        valid[:, pad_chunks:] = 1.0
        in_maps.append({
            "x_spans": x_spans, "valid": valid, "x_tm": x_tmc,
            "w_hbc": w_hbc, "w_gate": w_gate, "w_out": w_out, "w_base": w_base,
            "w_rout": w_rout, "w_A": w_A, "w_B": w_B,
            "conv_w": conv_w, "conv_b": conv_b, "dt_bias": dt_bias, "A_log": A_log,
            "D_bc": D_bc, "b_base": b_base, "norm_w": norm_w,
        })

    nc = build_program()
    res = run_bass_kernel_spmd(nc, in_maps, core_ids=list(range(8)))
    out = np.empty((2, 4096, 4096), np.float32)
    for core in range(8):
        b, q = core // 4, core % 4
        out[b, q * 1024:(q + 1) * 1024, :] = res.results[core]["out"]
    return out
